# version 06 (final)
# baseline (speedup 1.0000x reference)
import math
import numpy as np
from contextlib import ExitStack

import concourse.bass as bass
import concourse.mybir as mybir
from concourse.bass_utils import run_bass_kernel_spmd

F32 = mybir.dt.float32
BF16 = mybir.dt.bfloat16
I32 = mybir.dt.int32
AF = mybir.ActivationFunctionType
ALU = mybir.AluOpType
AX = mybir.AxisListType.X

H = 2048
SQ = 2048
SK = 4096
HC = H // 128
INTER = 5504
ICN = INTER // 128
CPC = 17
QB = 256
NQB = SQ // QB
EPS = 1e-5
ISQ = 1.0 / math.sqrt(2048.0)
NEWTON_ITERS = 18

ENGINES = ("tensor", "vector", "scalar", "gpsimd", "sync")


class Prog:

    def __init__(self):
        self.ev = {e: [] for e in ENGINES}
        self.cnt = {}
        self.waits = []

    def emit(self, eng, fn):
        self.ev[eng].append(fn)

    def bump(self, sem, n):
        v = self.cnt.get(sem, 0) + n
        self.cnt[sem] = v
        return v

    def val(self, sem):
        return self.cnt.get(sem, 0)

    def op(self, eng, build, inc=None):
        if inc is None:
            self.emit(eng, lambda E, S, build=build: build(E, S))
            return None
        sem, n = inc
        v = self.bump(sem, n)
        self.emit(eng, lambda E, S, build=build, sem=sem, n=n:
                  build(E, S).then_inc(S[sem], n))
        return v

    def group(self, eng, builds, inc):
        for b in builds[:-1]:
            self.emit(eng, lambda E, S, b=b: b(E, S))
        return self.op(eng, builds[-1], inc=inc)

    def dma(self, eng, sem, out, in_, **kw):
        self.wait(eng, sem, self.val(sem))
        return self.op(eng, lambda E, S, out=out, in_=in_, kw=kw:
                       E.dma_start(out=out, in_=in_, **kw), inc=(sem, 16))

    def dmat(self, eng, sem, out, in_):
        return self.op(eng, lambda E, S, out=out, in_=in_:
                       E.dma_start(out=out, in_=in_, transpose=True), inc=(sem, 16))

    def wait(self, eng, sem, v):
        if v is None or v <= 0:
            return
        self.waits.append((eng, sem, v))
        self.emit(eng, lambda E, S, sem=sem, v=v: E.wait_ge(S[sem], v))

    def vrt(self, build):
        v = self.op("vector", build, inc=("vx", 1))
        self.wait("vector", "vx", v)
        return v

    def validate(self):
        for eng, sem, v in self.waits:
            assert v <= self.val(sem), \
                f"wait on {sem}>={v} but final planned value is {self.val(sem)} ({eng})"

    def replay(self, block, sems):
        self.validate()

        def mk(eng):
            def body(E):
                for fn in self.ev[eng]:
                    fn(E, sems)
            return body
        block.tensor(mk("tensor"))
        block.vector(mk("vector"))
        block.scalar(mk("scalar"))
        block.gpsimd(mk("gpsimd"))
        block.sync(mk("sync"))


def build_nc(debug=False):
    heavy = (debug is True)
    nc = bass.Bass("TRN2", target_bir_lowering=False)

    xq = nc.dram_tensor("xq", [SQ, H], F32, kind="ExternalInput").ap()
    xqT = nc.dram_tensor("xqT", [H, SQ], F32, kind="ExternalInput").ap()
    xoT = nc.dram_tensor("xoT", [H, SQ], F32, kind="ExternalInput").ap()
    wqkT = nc.dram_tensor("wqkT", [32, H, 128], F32, kind="ExternalInput").ap()
    wvT = nc.dram_tensor("wvT", [HC, 128, H], F32, kind="ExternalInput").ap()
    bqkv = nc.dram_tensor("bqkv", [1, 3 * H], F32, kind="ExternalInput").ap()
    woT = nc.dram_tensor("woT", [HC, 128, H], F32, kind="ExternalInput").ap()
    bout = nc.dram_tensor("bout", [1, H], F32, kind="ExternalInput").ap()
    lut128 = nc.dram_tensor("lut128", [128, 16], F32, kind="ExternalInput").ap()
    idxs = nc.dram_tensor("idxs", [CPC, 128, H], I32, kind="ExternalInput").ap()

    out = nc.dram_tensor("out", [SQ, H], F32, kind="ExternalOutput").ap()

    kd = "ExternalOutput" if heavy else "Internal"
    q_sp = nc.dram_tensor("q_sp", [H, SQ], BF16, kind=kd).ap()
    k_sp = nc.dram_tensor("k_sp", [H, SK], BF16, kind=kd).ap()
    v_sp = nc.dram_tensor("v_sp", [SK, H], BF16, kind=kd).ap()
    p_sp = nc.dram_tensor("p_sp", [SQ, SK], BF16, kind=kd).ap()
    y1_sp = nc.dram_tensor("y1_sp", [SQ, H], F32, kind=kd).ap()
    y1b_sp = nc.dram_tensor("y1b_sp", [SQ, H], BF16, kind="Internal").ap()
    ag_in_t = nc.dram_tensor("ag_in", [CPC * 128, H], BF16, kind="Internal")
    ag_out_t = nc.dram_tensor("ag_out", [8 * CPC * 128, H], BF16, kind="Internal")
    ag_out = ag_out_t.ap()
    ag_dbg = (nc.dram_tensor("ag_dbg", [8 * CPC * 128, H], BF16,
                             kind="ExternalOutput").ap() if heavy else None)
    s_dbg = (nc.dram_tensor("s_dbg", [SQ, SK], F32,
                            kind="ExternalOutput").ap() if heavy else None)
    sm_dbg = (nc.dram_tensor("sm_dbg", [SQ, 4], F32,
                             kind="ExternalOutput").ap() if heavy else None)
    z_dbg = (nc.dram_tensor("z_dbg", [SQ, H], F32,
                            kind="ExternalOutput").ap() if heavy else None)
    f_dbg = (nc.dram_tensor("f_dbg", [SQ, H], F32,
                            kind="ExternalOutput").ap() if heavy else None)
    h2_dbg = (nc.dram_tensor("h2_dbg", [4 * 128, ICN * 512], BF16,
                             kind="ExternalOutput").ap() if heavy else None)
    yf_dbg = (nc.dram_tensor("yf_dbg", [128, HC * SQ], BF16,
                             kind="ExternalOutput").ap() if heavy else None)
    ctx_dbg = (nc.dram_tensor("ctx_dbg", [NQB * 128, HC * QB], BF16,
                              kind="ExternalOutput").ap() if debug else None)
    pt_dbg = (nc.dram_tensor("pt_dbg", [NQB * 128, 32 * QB], BF16,
                             kind="ExternalOutput").ap() if debug else None)

    P = Prog()
    sem_names = ["qs", "qg", "qc", "mm", "vx", "sc", "cc", "bi",
                 "g0", "g1", "a0", "a1", "xf", "w0", "w1", "s0", "s1",
                 "v0", "v1", "t0", "t1", "qb", "k0", "k1", "vb0", "vb1",
                 "wo0", "wo1", "xr0", "xr1", "ys", "yb", "yl0", "yl1",
                 "gu0", "gu1", "wd0", "wd1"]

    with ExitStack() as ctx:
        EE = ctx.enter_context
        EE(nc.reset_on_exit())

        ones = EE(nc.sbuf_tensor("ones", [1, 512], BF16))
        bo_sb = EE(nc.sbuf_tensor("bo_sb", [1, H], BF16))
        stats = EE(nc.sbuf_tensor("stats", [128, 8], F32))
        prime = EE(nc.sbuf_tensor("prime", [1, 1], F32))
        iota_f = EE(nc.sbuf_tensor("iota_f", [128, 128], F32))
        pidx_f = EE(nc.sbuf_tensor("pidx_f", [128, 1], F32))
        ident_f = EE(nc.sbuf_tensor("ident_f", [128, 128], F32))
        ident_b = EE(nc.sbuf_tensor("ident_b", [128, 128], BF16))
        y1t = EE(nc.sbuf_tensor("y1t", [128, H], F32))
        y1bt = EE(nc.sbuf_tensor("y1bt", [128, H], BF16))
        junk = y1bt
        xrt = EE(nc.sbuf_tensor("xrt", [128, 2, H], F32))
        lnst = EE(nc.sbuf_tensor("lnst", [128, 8], F32))
        gctx = EE(ExitStack())
        gidx = gctx.enter_context(nc.sbuf_tensor("gidx", [128, 2, H], I32))
        gidxb = gctx.enter_context(nc.sbuf_tensor("gidxb", [128, H], BF16))
        gtmp = gctx.enter_context(nc.sbuf_tensor("gtmp", [128, H], BF16))
        gw = gctx.enter_context(nc.sbuf_tensor("gw", [128, 2, H], BF16))
        lut_sb = gctx.enter_context(nc.sbuf_tensor("lut_sb", [128, 16], F32))

        psum = EE(nc.psum_tensor("psum", [128, 8, 512], F32))

        sems = {}
        for s in sem_names:
            sems[s] = EE(nc.semaphore(s))

        block = EE(nc.Block())

        P.op("scalar", lambda E, S: E.preload_act_table(AF.Exp))
        P.op("gpsimd", lambda E, S: E.memset(prime[:, :], 0.0))
        P.op("gpsimd", lambda E, S: E.memset(ones[:, :], 1.0))
        P.op("gpsimd", lambda E, S: E.iota(iota_f[:, :], [[1, 128]], channel_multiplier=0,
                                           allow_small_or_imprecise_dtypes=True))
        gp_pre = P.op("gpsimd", lambda E, S: E.iota(pidx_f[:, :], [[0, 1]],
                                                    channel_multiplier=1,
                                                    allow_small_or_imprecise_dtypes=True),
                      inc=("vx", 1))
        P.dma("gpsimd", "bi", bo_sb[:, :], bout)
        qg_bias = P.dma("gpsimd", "bi", lut_sb[:, :], lut128)
        P.wait("scalar", "vx", gp_pre)
        P.op("scalar", lambda E, S: E.activation(prime[:, :], prime[:, :], AF.Exp))
        P.wait("vector", "vx", gp_pre)
        P.op("vector", lambda E, S: E.tensor_scalar(
            ident_f[:, :], iota_f[:, :], pidx_f[:, :], None, op0=ALU.is_equal))
        P.op("vector", lambda E, S: E.tensor_scalar(
            ident_b[:, :], iota_f[:, :], pidx_f[:, :], None, op0=ALU.is_equal),
            inc=("vx", 1))
        P.wait("tensor", "vx", gp_pre)
        P.wait("tensor", "bi", qg_bias)
        P.wait("vector", "bi", qg_bias)

        gst = {"k": 0, "loads": [None] * CPC, "spills": [None] * CPC,
               "done": [None] * CPC}

        def gather_load(k):
            if k < CPC and gst["loads"][k] is None:
                if k >= 2:
                    P.wait("gpsimd", "vx", gst["done"][k - 2])
                gst["loads"][k] = P.dma("gpsimd", "g%d" % (k % 2),
                                        gidx[:, k % 2, :], idxs[k])

        def gather_chunk():
            k = gst["k"]
            if k >= CPC:
                return
            b = k % 2
            if k >= 2:
                P.wait("vector", "a%d" % (k % 2), gst["spills"][k - 2])
            P.wait("vector", "g%d" % (k % 2), gst["loads"][k])
            P.vrt(lambda E, S, b=b: E.tensor_copy(gidxb[:, :], gidx[:, b, :]))
            P.vrt(lambda E, S, b=b:
                  E.tensor_scalar(gw[:, b, :], gidxb[:, :], 0.0, lut_sb[:, 0:1],
                                  op0=ALU.is_equal, op1=ALU.mult))
            vg = None
            for j in range(1, 16):
                P.vrt(lambda E, S, j=j:
                      E.tensor_scalar(gtmp[:, :], gidxb[:, :], float(j),
                                      lut_sb[:, j:j + 1],
                                      op0=ALU.is_equal, op1=ALU.mult))
                vg = P.vrt(lambda E, S, b=b:
                           E.tensor_add(gw[:, b, :], gw[:, b, :], gtmp[:, :]))
            gst["done"][k] = vg
            P.wait("sync", "vx", vg)
            gst["spills"][k] = P.dma("sync", "a%d" % (k % 2),
                                     ag_in_t.ap()[k * 128:(k + 1) * 128, :], gw[:, b, :])
            gst["k"] = k + 1
            gather_load(k + 2)

        gather_load(0)
        gather_load(1)

        with ExitStack() as c1:
            E1 = c1.enter_context
            bq_sb = E1(nc.sbuf_tensor("bq_sb", [1, 3 * H], BF16))
            x_ft = E1(nc.sbuf_tensor("x_ft", [128, HC, SQ], BF16))
            wqk_b = E1(nc.sbuf_tensor("wqk_b", [128, 2, HC, 128], BF16))
            wv_b = E1(nc.sbuf_tensor("wv_b", [128, 2, HC, 512], BF16))
            st_qk = E1(nc.sbuf_tensor("st_qk", [128, 2, 4, 512], BF16))
            st_v = E1(nc.sbuf_tensor("st_v", [128, 2, 512], BF16))

            bq_ld = P.dma("gpsimd", "bi", bq_sb[:, :], bqkv)
            P.wait("tensor", "bi", bq_ld)

            wqk_loaded = [0, 0]
            slot_spill = [0, 0]
            bank_evac = [0, 0]
            wv_loaded = [0, 0]
            wv_slot_free = [0, 0]
            vst_spill = [0, 0]
            vbank_evac = [0, 0]
            xft_ready = [0]
            mega_tasks = {}
            qk_i = {"n": 0}
            v_i = {"n": 0}

            def mk_tasks(m):
                ocs = list(range(32)) if m == 0 else list(range(16, 32))
                t = []
                vq = [("v", vcol) for vcol in range(4)]
                for idx, oc in enumerate(ocs):
                    t.append(("qk", oc))
                    if idx % 8 == 7 and vq:
                        t.append(vq.pop(0))
                t += vq
                return t

            qk_queue = []
            for m in (0, 1):
                for kind, a in mk_tasks(m):
                    if kind == "qk":
                        qk_queue.append((m, a))
            v_queue = [(0, v) for v in range(4)] + [(1, v) for v in range(4)]

            def load_wqk(i):
                if i >= len(qk_queue):
                    return
                m, oc = qk_queue[i]
                slot = i % 2
                P.wait("gpsimd", "s%d" % slot, slot_spill[slot])
                wqk_loaded[slot] = P.dma("gpsimd", "w%d" % slot, wqk_b[:, slot, :, :],
                                         wqkT[oc].rearrange("(c p) n -> p c n", p=128))

            def load_wv(j):
                if j >= len(v_queue):
                    return
                vcol = v_queue[j][1]
                slot = j % 2
                P.wait("gpsimd", "mm", wv_slot_free[slot])
                wv_loaded[slot] = P.dma(
                    "gpsimd", "v%d" % slot, wv_b[:, slot, :, :],
                    wvT.rearrange("c p n -> p c n")[:, :, vcol * 512:(vcol + 1) * 512])

            load_wqk(0)
            load_wqk(1)
            load_wv(0)
            load_wv(1)

            def do_qk_task(m, oc):
                i = qk_i["n"]
                qk_i["n"] += 1
                slot = i % 2
                P.wait("tensor", "w%d" % slot, wqk_loaded[slot])
                P.wait("tensor", "xf", xft_ready[0])
                scv = 0
                for col in range(4):
                    pb = col % 2
                    P.wait("tensor", "sc", bank_evac[pb])
                    builds = [
                        (lambda E, S, pb=pb, slot=slot, hs=hs, col=col:
                         E.matmul(psum[:, pb, :], wqk_b[:, slot, hs, :],
                                  x_ft[:, hs, col * 512:(col + 1) * 512],
                                  start=(hs == 0), stop=False))
                        for hs in range(HC)]
                    builds.append(
                        lambda E, S, pb=pb, oc=oc:
                        E.matmul(psum[:, pb, :],
                                 bq_sb[0:1, oc * 128:(oc + 1) * 128],
                                 ones[0:1, :], start=False, stop=True))
                    mmv = P.group("tensor", builds, inc=("mm", 1))
                    P.wait("scalar", "mm", mmv)
                    if col == 0:
                        P.wait("scalar", "s%d" % slot, slot_spill[slot])
                    scv = P.op("scalar", lambda E, S, slot=slot, col=col, pb=pb:
                               E.activation(st_qk[:, slot, col, :], psum[:, pb, :],
                                            AF.Copy),
                               inc=("sc", 1))
                    bank_evac[pb] = scv
                if oc < 16:
                    dst = q_sp[oc * 128:(oc + 1) * 128, :]
                else:
                    dst = k_sp[(oc - 16) * 128:(oc - 16 + 1) * 128, m * SQ:(m + 1) * SQ]
                P.wait("sync", "sc", scv)
                slot_spill[slot] = P.dma(
                    "sync", "s%d" % slot, dst,
                    st_qk[:, slot, :, :].rearrange("p a b -> p (a b)"))
                load_wqk(i + 2)

            def do_v_task(m, vcol):
                j = v_i["n"]
                v_i["n"] += 1
                slot = j % 2
                P.wait("tensor", "v%d" % slot, wv_loaded[slot])
                P.wait("tensor", "xf", xft_ready[0])
                for tc in range(16):
                    pb = 2 + (tc % 2)
                    P.wait("tensor", "sc", vbank_evac[tc % 2])
                    builds = [
                        (lambda E, S, pb=pb, hs=hs, tc=tc, slot=slot:
                         E.matmul(psum[:, pb, :],
                                  x_ft[:, hs, tc * 128:(tc + 1) * 128],
                                  wv_b[:, slot, hs, :],
                                  start=(hs == 0), stop=False))
                        for hs in range(HC)]
                    builds.append(
                        lambda E, S, pb=pb, vcol=vcol:
                        E.matmul(psum[:, pb, :], ones[0:1, 0:128],
                                 bq_sb[0:1, 2 * H + vcol * 512:2 * H + (vcol + 1) * 512],
                                 start=False, stop=True))
                    mmv = P.group("tensor", builds, inc=("mm", 1))
                    if tc == 15:
                        wv_slot_free[slot] = mmv
                    sslot = tc % 2
                    P.wait("scalar", "mm", mmv)
                    P.wait("scalar", "t%d" % sslot, vst_spill[sslot])
                    scv = P.op("scalar", lambda E, S, sslot=sslot, pb=pb:
                               E.activation(st_v[:, sslot, :], psum[:, pb, :], AF.Copy),
                               inc=("sc", 1))
                    vbank_evac[tc % 2] = scv
                    P.wait("sync", "sc", scv)
                    vst_spill[sslot] = P.dma(
                        "sync", "t%d" % sslot,
                        v_sp[m * SQ + tc * 128:m * SQ + (tc + 1) * 128,
                             vcol * 512:(vcol + 1) * 512],
                        st_v[:, sslot, :])
                load_wv(j + 2)

            for m in (0, 1):
                if m == 1:
                    P.wait("gpsimd", "mm", P.val("mm"))
                src = xqT if m == 0 else xoT
                for hc in range(HC):
                    v = P.dma("gpsimd", "xf", x_ft[:, hc, :],
                              src[hc * 128:(hc + 1) * 128, :])
                xft_ready[0] = v
                for ti, (kind, a) in enumerate(mk_tasks(m)):
                    if kind == "qk":
                        do_qk_task(m, a)
                    else:
                        do_v_task(m, a)
                    if ti % 3 == 0:
                        gather_chunk()

            p1_fin = {s: P.val(s) for s in ("s0", "s1", "t0", "t1")}
            p1_sc = P.val("sc")

        while gst["k"] < CPC:
            gather_chunk()
        gather_fin = {s: P.val(s) for s in ("a0", "a1")}
        gctx.close()

        def emit_collective():
            P.wait("gpsimd", "a%d" % ((CPC - 1) % 2), gst["spills"][CPC - 1])
            P.wait("gpsimd", "a%d" % (CPC % 2), gst["spills"][CPC - 2])
            P.op("gpsimd", lambda E, S: E.collective_compute(
                "AllGather", ALU.bypass, replica_groups=[list(range(8))],
                ins=[ag_in_t.ap().opt()], outs=[ag_out_t.ap().opt()]),
                inc=("cc", 1))
        COLLECTIVE_EARLY = False
        def emit_ag_dbg():
            P.wait("gpsimd", "cc", 1)
            for ch in range(8):
                P.dma("gpsimd", "qg",
                      ag_dbg[ch * CPC * 128:(ch + 1) * CPC * 128, :],
                      ag_out[ch * CPC * 128:(ch + 1) * CPC * 128, :])

        if COLLECTIVE_EARLY:
            emit_collective()
            if heavy:
                emit_ag_dbg()

        epi = {"xr_vx": [0, 0], "y1t_dma": (0, 0), "zf_sc": 0}

        def ln_emit(row128, z_ap, to_out, z_gate=None):
            if z_gate:
                P.wait("vector", "vx", z_gate)
            rv = P.vrt(lambda E, S, z_ap=z_ap:
                       E.reduce_sum(lnst[:, 0:1], z_ap, axis=AX))
            P.wait("scalar", "vx", rv)
            P.wait("scalar", "yb", epi["y1t_dma"][1])
            sv = P.op("scalar", lambda E, S, z_ap=z_ap:
                      E.activation(junk[:, :], z_ap, AF.Square,
                                   accum_out=lnst[:, 2:3]), inc=("sc", 1))
            epi["zf_sc"] = sv
            P.wait("vector", "sc", sv)
            P.vrt(lambda E, S:
                  E.tensor_scalar(lnst[:, 1:2], lnst[:, 0:1], 1.0 / H, None,
                                  op0=ALU.mult))
            P.vrt(lambda E, S:
                  E.tensor_mul(lnst[:, 5:6], lnst[:, 1:2], lnst[:, 1:2]))
            P.vrt(lambda E, S:
                  E.tensor_scalar(lnst[:, 3:4], lnst[:, 2:3], 1.0 / H, EPS,
                                  op0=ALU.mult, op1=ALU.add))
            P.vrt(lambda E, S:
                  E.tensor_sub(lnst[:, 3:4], lnst[:, 3:4], lnst[:, 5:6]))
            P.vrt(lambda E, S: E.reciprocal(lnst[:, 4:5], lnst[:, 3:4]))
            P.vrt(lambda E, S:
                  E.tensor_scalar(lnst[:, 4:5], lnst[:, 4:5], 1.0, None, op0=ALU.min))
            for _ in range(NEWTON_ITERS):
                P.vrt(lambda E, S:
                      E.tensor_mul(lnst[:, 5:6], lnst[:, 4:5], lnst[:, 4:5]))
                P.vrt(lambda E, S:
                      E.tensor_mul(lnst[:, 5:6], lnst[:, 5:6], lnst[:, 3:4]))
                P.vrt(lambda E, S:
                      E.tensor_scalar(lnst[:, 5:6], lnst[:, 5:6], -0.5, 1.5,
                                      op0=ALU.mult, op1=ALU.add))
                P.vrt(lambda E, S:
                      E.tensor_mul(lnst[:, 4:5], lnst[:, 4:5], lnst[:, 5:6]))
            P.vrt(lambda E, S:
                  E.tensor_mul(lnst[:, 6:7], lnst[:, 1:2], lnst[:, 4:5]))
            P.vrt(lambda E, S:
                  E.tensor_scalar(lnst[:, 6:7], lnst[:, 6:7], -1.0, None,
                                  op0=ALU.mult))
            P.wait("vector", "ys", epi["y1t_dma"][0])
            P.wait("vector", "yb", epi["y1t_dma"][1])
            P.vrt(lambda E, S, z_ap=z_ap:
                  E.tensor_scalar(y1t[:, :], z_ap, lnst[:, 4:5], lnst[:, 6:7],
                                  op0=ALU.mult, op1=ALU.add))
            vv = P.op("vector", lambda E, S:
                      E.tensor_copy(y1bt[:, :], y1t[:, :]), inc=("vx", 1))
            P.wait("scalar", "vx", vv)
            if to_out:
                epi["y1t_dma"] = (
                    P.dma("scalar", "ys", out[row128 * 128:(row128 + 1) * 128, :],
                          y1t[:, :]),
                    epi["y1t_dma"][1])
            else:
                ya = P.dma("scalar", "ys",
                           y1_sp[row128 * 128:(row128 + 1) * 128, :], y1t[:, :])
                yb_ = P.dma("scalar", "yb",
                            y1b_sp[row128 * 128:(row128 + 1) * 128, :], y1bt[:, :])
                epi["y1t_dma"] = (ya, yb_)

        with ExitStack() as c2:
            E2 = c2.enter_context
            q_b = E2(nc.sbuf_tensor("q_b", [128, HC, QB], BF16))
            k_b = E2(nc.sbuf_tensor("k_b", [128, 2, HC, 256], BF16))
            s_sb = E2(nc.sbuf_tensor("s_sb", [128, 2, SK], F32))
            p_b = E2(nc.sbuf_tensor("p_b", [128, 2, SK], BF16))
            pt_b = E2(nc.sbuf_tensor("pt_b", [128, 32, QB], BF16))
            v_b = E2(nc.sbuf_tensor("v_b", [128, 2, 4, 512], BF16))
            ctx_b = E2(nc.sbuf_tensor("ctx_b", [128, HC, QB], BF16))
            wo_b = E2(nc.sbuf_tensor("wo_b", [128, 2, 8, 512], BF16))
            sums = E2(nc.sbuf_tensor("sums", [128, 2, 1], F32))
            rden = E2(nc.sbuf_tensor("rden", [128, 2, 1], F32))
            zt = E2(nc.sbuf_tensor("zt", [128, 2, H], F32))

            st = {
                "scores_sc_end": [0] * NQB,
                "exp_done": [[0, 0] for _ in range(NQB)],
                "pnorm_done": [[0, 0] for _ in range(NQB)],
                "pt_done": [[0] * 8 for _ in range(NQB)],
                "pt_mm_end": [0] * NQB,
                "pt_evac_end": [0] * NQB,
                "ctx_mm_end": [0] * NQB,
                "ctx_evac": [0] * NQB,
                "op_done_vx": [0] * NQB,
                "rden_done": [0] * NQB,
                "q_free": [0] * NQB,
            }
            kslot_free = [0, 0]
            vslot_free = [0, 0]
            woslot_free = [0, 0]

            for eng in ("sync", "gpsimd"):
                for s_, v_ in p1_fin.items():
                    P.wait(eng, s_, v_)
                for s_ in ("a0", "a1"):
                    P.wait(eng, s_, gather_fin[s_])
            P.wait("tensor", "sc", p1_sc)

            def emit_scores(q):
                if q >= 1:
                    P.wait("sync", "mm", st["q_free"][q - 1])
                q_ready = P.dma(
                    "sync", "qb", q_b[:, :, :],
                    q_sp[:, q * QB:(q + 1) * QB].rearrange("(c p) n -> p c n", p=128))
                kld = [0] * 16
                for kg in range(2):
                    P.wait("sync", "mm", kslot_free[kg % 2])
                    kld[kg] = P.dma("sync", "k%d" % (kg % 2), k_b[:, kg % 2, :, :],
                                    k_sp[:, kg * 256:(kg + 1) * 256]
                                    .rearrange("(c p) n -> p c n", p=128))
                P.wait("tensor", "qb", q_ready)
                if q >= 1:
                    P.wait("tensor", "sc", st["scores_sc_end"][q - 1])
                    P.wait("tensor", "vx", st["pt_evac_end"][q - 1])
                last_evac = [0, 0]
                for kg in range(16):
                    slot = kg % 2
                    P.wait("tensor", "k%d" % slot, kld[kg])
                    for qs_i in range(2):
                        P.wait("tensor", "sc", last_evac[qs_i])
                        builds = [
                            (lambda E, S, qs_i=qs_i, hs=hs, slot=slot:
                             E.matmul(psum[:, qs_i, 0:256],
                                      q_b[:, hs, qs_i * 128:(qs_i + 1) * 128],
                                      k_b[:, slot, hs, :],
                                      start=(hs == 0), stop=(hs == HC - 1)))
                            for hs in range(HC)]
                        mmv = P.group("tensor", builds, inc=("mm", 1))
                        P.wait("scalar", "mm", mmv)
                        last_evac[qs_i] = P.op(
                            "scalar", lambda E, S, qs_i=qs_i, kg=kg:
                            E.activation(s_sb[:, qs_i, kg * 256:(kg + 1) * 256],
                                         psum[:, qs_i, 0:256], AF.Copy, scale=ISQ),
                            inc=("sc", 1))
                    kslot_free[slot] = P.val("mm")
                    if kg + 2 < 16:
                        P.wait("sync", "mm", kslot_free[slot])
                        kld[kg + 2] = P.dma("sync", "k%d" % slot, k_b[:, slot, :, :],
                                            k_sp[:, (kg + 2) * 256:(kg + 3) * 256]
                                            .rearrange("(c p) n -> p c n", p=128))
                st["scores_sc_end"][q] = P.val("sc")
                st["q_free"][q] = P.val("mm")

            def emit_softmax_pt(q):
                for qs_i in range(2):
                    P.wait("vector", "sc", st["scores_sc_end"][q] - (1 - qs_i))
                    P.vrt(lambda E, S, qs_i=qs_i:
                          E.reduce_max(stats[:, 2 * qs_i:2 * qs_i + 1],
                                       s_sb[:, qs_i, :], axis=AX))
                    P.vrt(lambda E, S, qs_i=qs_i:
                          E.tensor_scalar(stats[:, 2 * qs_i + 1:2 * qs_i + 2],
                                          stats[:, 2 * qs_i:2 * qs_i + 1],
                                          -1.0, None, op0=ALU.mult))
                    vxs = P.op("vector", lambda E, S, qs_i=qs_i:
                               E.tensor_scalar(s_sb[:, qs_i, :], s_sb[:, qs_i, :],
                                               stats[:, 2 * qs_i + 1:2 * qs_i + 2],
                                               None, op0=ALU.add),
                               inc=("vx", 1))
                    P.wait("scalar", "vx", vxs)
                    if q >= 1:
                        P.wait("scalar", "mm", st["pt_mm_end"][q - 1])
                    scv = P.op("scalar", lambda E, S, qs_i=qs_i:
                               E.activation(p_b[:, qs_i, :], s_sb[:, qs_i, :], AF.Exp),
                               inc=("sc", 1))
                    st["exp_done"][q][qs_i] = scv
                    P.wait("vector", "sc", scv)
                    P.vrt(lambda E, S, qs_i=qs_i:
                          E.reduce_sum(sums[:, qs_i, :], p_b[:, qs_i, :], axis=AX))
                    st["rden_done"][q] = P.op(
                        "vector", lambda E, S, qs_i=qs_i:
                        E.reciprocal(rden[:, qs_i, :], sums[:, qs_i, :]),
                        inc=("vx", 1))
                    if heavy:
                        P.wait("scalar", "sc", scv)
                        r0 = (q * 2 + qs_i) * 128
                        st["p_dump"] = P.dma(
                            "scalar", "qc", p_sp[r0:r0 + 128, :], p_b[:, qs_i, :])
                        P.wait("vector", "qc", st["p_dump"])
                    P.wait("vector", "vx", st["rden_done"][q])
                    st["pnorm_done"][q][qs_i] = P.op(
                        "vector", lambda E, S, qs_i=qs_i:
                        E.tensor_scalar(p_b[:, qs_i, :], p_b[:, qs_i, :],
                                        rden[:, qs_i, :], None, op0=ALU.mult),
                        inc=("vx", 1))
                    if heavy:
                        P.dma("scalar", "qc", s_dbg[r0:r0 + 128, :], s_sb[:, qs_i, :])
                        P.dma("scalar", "qc", sm_dbg[r0:r0 + 128, 0:1],
                              stats[:, 2 * qs_i:2 * qs_i + 1],
                              allow_slow_non_contiguous=True)
                        P.wait("scalar", "vx", st["rden_done"][q])
                        P.dma("scalar", "qc", sm_dbg[r0:r0 + 128, 2:3],
                              sums[:, qs_i, :],
                              allow_slow_non_contiguous=True)

            def emit_pt(q):
                pb_bf = [psum[:, 0, 0:64].bitcast(BF16), psum[:, 1, 0:64].bitcast(BF16)]
                evac_gate = [0, 0]
                P.wait("tensor", "vx", max(st["pnorm_done"][q]))
                if q >= 1:
                    P.wait("vector", "mm", st["ctx_mm_end"][q - 1])
                if debug and q >= 1:
                    P.wait("vector", "qc", st.get("pt_dump", 0))
                for ktc in range(32):
                    for qs_i in range(2):
                        P.wait("tensor", "vx", evac_gate[qs_i])
                        mmv = P.op("tensor", lambda E, S, qs_i=qs_i, ktc=ktc:
                                   E.transpose(pb_bf[qs_i],
                                               p_b[:, qs_i, ktc * 128:(ktc + 1) * 128],
                                               ident_b[:, :]),
                                   inc=("mm", 1))
                        P.wait("vector", "mm", mmv)
                        evac_gate[qs_i] = P.op(
                            "vector", lambda E, S, qs_i=qs_i, ktc=ktc:
                            E.tensor_copy(pt_b[:, ktc, qs_i * 128:(qs_i + 1) * 128],
                                          pb_bf[qs_i]),
                            inc=("vx", 1))
                    if ktc % 4 == 3:
                        st["pt_done"][q][ktc // 4] = P.val("vx")
                st["pt_mm_end"][q] = P.val("mm")
                st["pt_evac_end"][q] = P.val("vx")

            def emit_ctx_op_epi(q):
                if q >= 1:
                    P.wait("tensor", "vx", st["ctx_evac"][q - 1])
                for hq in range(4):
                    if hq >= 1:
                        P.wait("tensor", "vx", st["ctx_evac"][q])
                    for g in range(8):
                        slot = g % 2
                        P.wait("gpsimd", "mm", vslot_free[slot])
                        vld = P.dma("gpsimd", "vb%d" % slot, v_b[:, slot, :, :],
                                    v_sp[g * 512:(g + 1) * 512, hq * 512:(hq + 1) * 512]
                                    .rearrange("(a p) n -> p a n", p=128))
                        P.wait("tensor", "vb%d" % slot, vld)
                        P.wait("tensor", "vx", st["pt_done"][q][g])
                        builds = []
                        for jj in range(4):
                            ktc = g * 4 + jj
                            for hs in range(4):
                                builds.append(
                                    lambda E, S, hs=hs, slot=slot, jj=jj, ktc=ktc:
                                    E.matmul(psum[:, 2 + hs, 0:QB],
                                             v_b[:, slot, jj, hs * 128:(hs + 1) * 128],
                                             pt_b[:, ktc, :],
                                             start=(ktc == 0), stop=(ktc == 31)))
                        mmv = P.group("tensor", builds, inc=("mm", 1))
                        vslot_free[slot] = mmv
                    P.wait("vector", "mm", mmv)
                    if debug and hq == 0:
                        P.wait("vector", "qc", st.get("ctx_dump", 0))
                    cbuilds = [
                        (lambda E, S, hq=hq, hs=hs:
                         E.tensor_copy(ctx_b[:, hq * 4 + hs, :], psum[:, 2 + hs, 0:QB]))
                        for hs in range(4)]
                    st["ctx_evac"][q] = P.group("vector", cbuilds, inc=("vx", 1))
                st["ctx_mm_end"][q] = P.val("mm")

                xr_ld = [0, 0]
                for tc in range(2):
                    P.wait("gpsimd", "vx", epi["xr_vx"][tc])
                    xr_ld[tc] = P.dma("gpsimd", "xr%d" % tc, xrt[:, tc, :],
                                      xq[(q * 2 + tc) * 128:(q * 2 + tc + 1) * 128, :])

                last_z = st["op_done_vx"][q - 1] if q >= 1 else 0
                for ocol in range(4):
                    for hsh in range(2):
                        slot = hsh
                        P.wait("gpsimd", "mm", woslot_free[slot])
                        wov = P.dma("gpsimd", "wo%d" % slot, wo_b[:, slot, :, :],
                                    woT.rearrange("c p n -> p c n")
                                    [:, hsh * 8:(hsh + 1) * 8, ocol * 512:(ocol + 1) * 512])
                        P.wait("tensor", "wo%d" % slot, wov)
                        if hsh == 0:
                            P.wait("tensor", "vx", last_z)
                            if ocol == 0:
                                P.wait("tensor", "vx", st["ctx_evac"][q])
                        builds = []
                        for tc in range(2):
                            for i in range(8):
                                hs = hsh * 8 + i
                                builds.append(
                                    lambda E, S, tc=tc, hs=hs, slot=slot, i=i, hsh=hsh:
                                    E.matmul(psum[:, 6 + tc, :],
                                             ctx_b[:, hs, tc * 128:(tc + 1) * 128],
                                             wo_b[:, slot, i, :],
                                             start=(hsh == 0 and i == 0), stop=False))
                            if hsh == 1:
                                builds.append(
                                    lambda E, S, tc=tc, ocol=ocol:
                                    E.matmul(psum[:, 6 + tc, :], ones[0:1, 0:128],
                                             bo_sb[0:1, ocol * 512:(ocol + 1) * 512],
                                             start=False, stop=True))
                        mmv = P.group("tensor", builds, inc=("mm", 1))
                        woslot_free[slot] = mmv
                    P.wait("vector", "mm", mmv)
                    if ocol == 0:
                        P.wait("vector", "xr0", xr_ld[0])
                        P.wait("vector", "xr1", xr_ld[1])
                        P.wait("vector", "vx", st["rden_done"][q])
                        if q >= 1:
                            P.wait("vector", "sc", epi["zf_sc"])
                    zbuilds = [
                        (lambda E, S, tc=tc, ocol=ocol:
                         E.tensor_add(
                             zt[:, tc, ocol * 512:(ocol + 1) * 512],
                             psum[:, 6 + tc, :],
                             xrt[:, tc, ocol * 512:(ocol + 1) * 512]))
                        for tc in range(2)]
                    last_z = P.group("vector", zbuilds, inc=("vx", 1))
                st["op_done_vx"][q] = P.val("vx")
                if heavy:
                    P.wait("scalar", "vx", st["op_done_vx"][q])
                    for tc in range(2):
                        P.dma("scalar", "qc",
                              z_dbg[(q * 2 + tc) * 128:(q * 2 + tc + 1) * 128, :],
                              zt[:, tc, :])
                if debug:
                    P.wait("scalar", "vx", st["ctx_evac"][q])
                    st["ctx_dump"] = P.dma(
                        "scalar", "qc", ctx_dbg[q * 128:(q + 1) * 128, :],
                        ctx_b[:, :, :].rearrange("p c n -> p (c n)"))
                    if heavy or q == 3:
                        P.wait("scalar", "vx", st["pt_evac_end"][q])
                        st["pt_dump"] = P.dma(
                            "scalar", "qc", pt_dbg[q * 128:(q + 1) * 128, :],
                            pt_b[:, :, :].rearrange("p c n -> p (c n)"))

                for tc in range(2):
                    ln_emit(q * 2 + tc, zt[:, tc, :], to_out=False,
                            z_gate=st["op_done_vx"][q])
                epi["xr_vx"][0] = st["op_done_vx"][q]
                epi["xr_vx"][1] = st["op_done_vx"][q]

            PIPELINE = False
            if PIPELINE:
                emit_scores(0)
                emit_softmax_pt(0)
                emit_pt(0)
                for q in range(1, NQB):
                    emit_scores(q)
                    emit_ctx_op_epi(q - 1)
                    emit_softmax_pt(q)
                    emit_pt(q)
                emit_ctx_op_epi(NQB - 1)
            else:
                for q in range(NQB):
                    emit_scores(q)
                    emit_softmax_pt(q)
                    emit_pt(q)
                    emit_ctx_op_epi(q)

            p2_fin = {"ys": P.val("ys"), "yb": P.val("yb")}
            p2_sc = P.val("sc")
            p2_vx = P.val("vx")
        if not COLLECTIVE_EARLY:
            emit_collective()
            if heavy:
                emit_ag_dbg()

        with ExitStack() as c3:
            E3 = c3.enter_context
            y1_ft = E3(nc.sbuf_tensor("y1_ft", [128, HC, SQ], BF16))
            wg_b = E3(nc.sbuf_tensor("wg_b", [128, 2, HC, 128], BF16))
            wu_b = E3(nc.sbuf_tensor("wu_b", [128, 2, HC, 128], BF16))
            wd_b = E3(nc.sbuf_tensor("wd_b", [128, 2, 512], BF16))
            h2_b = E3(nc.sbuf_tensor("h2_b", [128, ICN, 512], BF16))
            facc = E3(nc.sbuf_tensor("facc", [128, HC, 512], F32))
            th_b = E3(nc.sbuf_tensor("th_b", [128, 512], F32))
            s1_b = th_b
            gs_b = E3(nc.sbuf_tensor("gs_b", [128, 512], F32))
            zf_b = E3(nc.sbuf_tensor("zf_b", [128, H], F32))

            P.wait("sync", "cc", 1)
            for eng in ("sync", "gpsimd"):
                for s_, v_ in p2_fin.items():
                    P.wait(eng, s_, v_)
            if debug:
                qc_now = P.val("qc")
                for eng in ("sync", "gpsimd", "vector", "scalar", "tensor"):
                    P.wait(eng, "qc", qc_now)
            P.wait("tensor", "sc", p2_sc)
            P.wait("tensor", "vx", p2_vx)
            ypb = [psum[:, 0, 0:64].bitcast(BF16), psum[:, 1, 0:64].bitcast(BF16)]
            ygate = [p2_vx, p2_vx]
            yld_v = [0, 0]
            y1ld = wg_b
            for rr in range(2):
                yld_v[rr] = P.dma("sync", "yl%d" % rr,
                                  y1ld[:, rr, :, :].rearrange("p a b -> p (a b)"),
                                  y1b_sp[rr * 128:(rr + 1) * 128, :])
            for r in range(16):
                sl = r % 2
                P.wait("tensor", "yl%d" % sl, yld_v[sl])
                last_tr = 0
                for hc in range(HC):
                    pb2 = hc % 2
                    P.wait("tensor", "vx", ygate[pb2])
                    mmv = P.op("tensor", lambda E, S, pb2=pb2, sl=sl, hc=hc:
                               E.transpose(ypb[pb2], y1ld[:, sl, hc, :],
                                           ident_b[:, :]),
                               inc=("mm", 1))
                    P.wait("vector", "mm", mmv)
                    ygate[pb2] = P.op(
                        "vector", lambda E, S, pb2=pb2, hc=hc, r=r:
                        E.tensor_copy(y1_ft[:, hc, r * 128:(r + 1) * 128], ypb[pb2]),
                        inc=("vx", 1))
                    last_tr = mmv
                if r + 2 < 16:
                    P.wait("sync", "mm", last_tr)
                    yld_v[sl] = P.dma("sync", "yl%d" % sl,
                                      y1ld[:, sl, :, :].rearrange("p a b -> p (a b)"),
                                      y1b_sp[(r + 2) * 128:(r + 3) * 128, :])
            yft_ready_vx = P.val("vx")
            yft_ready_mm = P.val("mm")
            P.wait("tensor", "vx", yft_ready_vx)
            if heavy:
                P.wait("scalar", "vx", yft_ready_vx)
                P.dma("scalar", "qc", yf_dbg[:, :],
                      y1_ft[:, :, :].rearrange("p c n -> p (c n)"))

            guslot_free = [yft_ready_mm, yft_ready_mm]
            gu_bank_free = [0, 0]
            wdslot_free = [0, 0]
            dn_bank_free = 0
            facc_free = 0
            h2_free = 0
            h2_done = [0] * ICN
            wg_ld = [0, 0]
            wu_ld = [0, 0]
            th_free = [0]
            gu_total = 4 * ICN

            def load_gu(t):
                if t >= gu_total:
                    return
                ic = t % ICN
                slot = t % 2
                P.wait("sync", "mm", guslot_free[slot])
                wg_ld[slot] = P.dma("sync", "gu%d" % slot, wg_b[:, slot, :, :],
                                    ag_out[ic * 128:(ic + 1) * 128, :]
                                    .rearrange("p (c n) -> p c n", c=HC))
                wu_ld[slot] = P.dma("sync", "gu%d" % slot, wu_b[:, slot, :, :],
                                    ag_out[(ICN + ic) * 128:(ICN + ic + 1) * 128, :]
                                    .rearrange("p (c n) -> p c n", c=HC))

            load_gu(0)
            load_gu(1)

            icgs = [list(range(0, 11)), list(range(11, 22)),
                    list(range(22, 33)), list(range(33, 43))]

            for tb in range(4):
                for ic in range(ICN):
                    t = tb * ICN + ic
                    slot = t % 2
                    par = ic % 2
                    P.wait("tensor", "gu%d" % slot, wu_ld[slot])
                    P.wait("tensor", "vx", gu_bank_free[par])
                    if tb >= 1 and ic == 0:
                        P.wait("tensor", "mm", h2_free)
                    builds = [
                        (lambda E, S, par=par, slot=slot, hs=hs, tb=tb:
                         E.matmul(psum[:, par, :], wg_b[:, slot, hs, :],
                                  y1_ft[:, hs, tb * 512:(tb + 1) * 512],
                                  start=(hs == 0), stop=(hs == HC - 1)))
                        for hs in range(HC)]
                    builds += [
                        (lambda E, S, par=par, slot=slot, hs=hs, tb=tb:
                         E.matmul(psum[:, 2 + par, :], wu_b[:, slot, hs, :],
                                  y1_ft[:, hs, tb * 512:(tb + 1) * 512],
                                  start=(hs == 0), stop=(hs == HC - 1)))
                        for hs in range(HC)]
                    mmv = P.group("tensor", builds, inc=("mm", 1))
                    guslot_free[slot] = mmv
                    load_gu(t + 2)
                    P.wait("scalar", "mm", mmv)
                    P.wait("scalar", "vx", th_free[0])
                    scv = P.op("scalar", lambda E, S, par=par:
                               E.activation(th_b[:, :], psum[:, par, :], AF.Tanh,
                                            scale=0.5),
                               inc=("sc", 1))
                    P.wait("vector", "sc", scv)
                    P.vrt(lambda E, S:
                          E.tensor_scalar(s1_b[:, :], th_b[:, :], 0.5, 0.5,
                                          op0=ALU.mult, op1=ALU.add))
                    th_free[0] = P.vrt(lambda E, S, par=par:
                                       E.tensor_mul(gs_b[:, :], psum[:, par, :],
                                                    s1_b[:, :]))
                    hv = P.op("vector", lambda E, S, ic=ic, par=par:
                              E.tensor_mul(h2_b[:, ic, :], gs_b[:, :], psum[:, 2 + par, :]),
                              inc=("vx", 1))
                    gu_bank_free[par] = hv
                    h2_done[ic] = hv

                if tb >= 1:
                    P.wait("vector", "mm", facc_free)
                for hog in range(4):
                    for gi, icg in enumerate(icgs):
                        P.wait("tensor", "vx", dn_bank_free)
                        if hog == 0:
                            P.wait("tensor", "vx", h2_done[icg[-1]])
                        builds = []
                        wd_waits = []
                        for ic in icg:
                            slot = ic % 2
                            P.wait("sync", "mm", wdslot_free[slot])
                            wdv = P.dma(
                                "sync", "wd%d" % slot, wd_b[:, slot, :],
                                ag_out[(2 * ICN + ic) * 128:(2 * ICN + ic + 1) * 128,
                                       hog * 512:(hog + 1) * 512])
                            P.wait("tensor", "wd%d" % slot, wdv)
                            ic_builds = [
                                (lambda E, S, hc4=hc4, slot=slot, ic=ic, tb=tb,
                                 first=(ic == icg[0]), last=(ic == icg[-1]):
                                 E.matmul(psum[:, 4 + hc4, :],
                                          wd_b[:, slot, hc4 * 128:(hc4 + 1) * 128],
                                          h2_b[:, ic, :],
                                          start=first, stop=last))
                                for hc4 in range(4)]
                            mmv = P.group("tensor", ic_builds, inc=("mm", 1))
                            wdslot_free[slot] = mmv
                        P.wait("vector", "mm", mmv)
                        ebuilds = []
                        for hc4 in range(4):
                            hoc = hog * 4 + hc4
                            if gi == 0:
                                ebuilds.append(
                                    lambda E, S, hoc=hoc, hc4=hc4:
                                    E.tensor_copy(facc[:, hoc, :], psum[:, 4 + hc4, :]))
                            else:
                                ebuilds.append(
                                    lambda E, S, hoc=hoc, hc4=hc4:
                                    E.tensor_add(facc[:, hoc, :], facc[:, hoc, :],
                                                 psum[:, 4 + hc4, :]))
                        dn_bank_free = P.group("vector", ebuilds, inc=("vx", 1))
                        P.wait("vector", "vx", dn_bank_free)
                h2_free = P.val("mm")

                tr_gate = [dn_bank_free, dn_bank_free]
                for tsub in range(4):
                    row = tb * 4 + tsub
                    xslot = tsub % 2
                    P.wait("gpsimd", "vx", epi["xr_vx"][xslot])
                    xrv = P.dma("gpsimd", "xr%d" % xslot, xrt[:, xslot, :],
                                y1_sp[row * 128:(row + 1) * 128, :])
                    P.wait("vector", "sc", epi["zf_sc"])
                    if heavy:
                        P.wait("vector", "qc", st.get("f_dump", 0))
                    for hoc in range(HC):
                        pb = 4 + (hoc % 2)
                        P.wait("tensor", "vx", tr_gate[hoc % 2])
                        mmv = P.op("tensor", lambda E, S, pb=pb, hoc=hoc, tsub=tsub:
                                   E.transpose(psum[:, pb, 0:128],
                                               facc[:, hoc, tsub * 128:(tsub + 1) * 128],
                                               ident_f[:, :]),
                                   inc=("mm", 1))
                        P.wait("vector", "mm", mmv)
                        tr_gate[hoc % 2] = P.op(
                            "vector", lambda E, S, pb=pb, hoc=hoc:
                            E.tensor_copy(zf_b[:, hoc * 128:(hoc + 1) * 128],
                                          psum[:, pb, 0:128]),
                            inc=("vx", 1))
                    P.wait("vector", "xr%d" % xslot, xrv)
                    P.wait("vector", "vx", max(tr_gate))
                    vv = P.op("vector", lambda E, S, xslot=xslot:
                              E.tensor_add(zf_b[:, :], zf_b[:, :], xrt[:, xslot, :]),
                              inc=("vx", 1))
                    P.wait("vector", "vx", vv)
                    epi["xr_vx"][xslot] = vv
                    if heavy:
                        P.wait("scalar", "vx", vv)
                        st["f_dump"] = P.dma(
                            "scalar", "qc", f_dbg[row * 128:(row + 1) * 128, :],
                            zf_b[:, :])
                    ln_emit(row, zf_b[:, :], to_out=True, z_gate=vv)
                facc_free = P.val("mm")
                dn_bank_free = max(tr_gate)

            P.wait("scalar", "ys", P.val("ys"))
            P.wait("scalar", "yb", P.val("yb"))
            if debug:
                P.wait("scalar", "qc", P.val("qc"))
                P.wait("gpsimd", "qg", P.val("qg"))

        P.replay(block, sems)

    return nc



def _pack_idx_chunk(g, gate_idx, up_idx, down_idx):
    if g < ICN:
        ic = g
        A = gate_idx[ic * 128:(ic + 1) * 128, :]
        return np.ascontiguousarray(
            A.reshape(128, HC, 128).transpose(2, 1, 0).reshape(128, H))
    if g < 2 * ICN:
        ic = g - ICN
        A = up_idx[ic * 128:(ic + 1) * 128, :]
        return np.ascontiguousarray(
            A.reshape(128, HC, 128).transpose(2, 1, 0).reshape(128, H))
    if g < 3 * ICN:
        ic = g - 2 * ICN
        return np.ascontiguousarray(down_idx[:, ic * 128:(ic + 1) * 128].T)
    return np.zeros((128, H), np.int32)


_NC_CACHE = {}


def make_in_maps(ins):
    x = np.ascontiguousarray(np.asarray(ins["L0_attn_hidden_states"],
                                        dtype=np.float32))
    w_qkv = np.asarray(ins["L0_attn_w_qkv"], dtype=np.float32)
    b_qkv = np.asarray(ins["L0_attn_b_qkv"], dtype=np.float32)
    w_out = np.asarray(ins["L0_attn_w_out"], dtype=np.float32)
    b_out = np.asarray(ins["L0_attn_b_out"], dtype=np.float32)
    lut = np.asarray(ins["L0_ffn_lut"], dtype=np.float32)
    gate_idx = np.asarray(ins["L0_ffn_gate_idx"], dtype=np.int32)
    up_idx = np.asarray(ins["L0_ffn_up_idx"], dtype=np.int32)
    down_idx = np.asarray(ins["L0_ffn_down_idx"], dtype=np.int32)

    wT = np.ascontiguousarray(w_qkv.T)
    wqkT = np.ascontiguousarray(
        wT[:, :2 * H].reshape(H, 32, 128).transpose(1, 0, 2))
    wvT = np.ascontiguousarray(
        wT[:, 2 * H:].reshape(HC, 128, H))
    woT = np.ascontiguousarray(w_out.T.reshape(HC, 128, H))
    bqkv2 = np.ascontiguousarray(b_qkv.reshape(1, 3 * H))
    bout2 = np.ascontiguousarray(b_out.reshape(1, H))
    lutb = np.ascontiguousarray(np.broadcast_to(lut.reshape(1, 16), (128, 16)))

    chunks = [_pack_idx_chunk(g, gate_idx, up_idx, down_idx)
              for g in range(8 * CPC)]

    in_maps = []
    for c in range(8):
        b, half = c // 2, c % 2
        xqc = np.ascontiguousarray(x[b, half * SQ:(half + 1) * SQ])
        xqTc = np.ascontiguousarray(xqc.T)
        xoTc = np.ascontiguousarray(x[b, (1 - half) * SQ:(2 - half) * SQ].T)
        idxsc = np.ascontiguousarray(
            np.stack(chunks[c * CPC:(c + 1) * CPC]))
        in_maps.append({
            "xq": xqc, "xqT": xqTc, "xoT": xoTc,
            "wqkT": wqkT, "wvT": wvT, "bqkv": bqkv2,
            "woT": woT, "bout": bout2, "lut128": lutb,
            "idxs": idxsc,
        })
    return in_maps


def kernel(L0_attn_hidden_states, L0_attn_w_qkv, L0_attn_b_qkv, L0_attn_w_out,
           L0_attn_b_out, L0_ffn_lut, L0_ffn_gate_idx, L0_ffn_up_idx,
           L0_ffn_down_idx):
    x = np.asarray(L0_attn_hidden_states)
    B, S, _ = x.shape
    in_maps = make_in_maps({
        "L0_attn_hidden_states": x,
        "L0_attn_w_qkv": L0_attn_w_qkv, "L0_attn_b_qkv": L0_attn_b_qkv,
        "L0_attn_w_out": L0_attn_w_out, "L0_attn_b_out": L0_attn_b_out,
        "L0_ffn_lut": L0_ffn_lut, "L0_ffn_gate_idx": L0_ffn_gate_idx,
        "L0_ffn_up_idx": L0_ffn_up_idx, "L0_ffn_down_idx": L0_ffn_down_idx,
    })

    key = "nc"
    if key not in _NC_CACHE:
        _NC_CACHE[key] = build_nc()
    nc = _NC_CACHE[key]

    res = run_bass_kernel_spmd(nc, in_maps, core_ids=list(range(8)))

    outp = np.empty((B, S, H), np.float32)
    for c in range(8):
        b, half = c // 2, c % 2
        outp[b, half * SQ:(half + 1) * SQ] = res.results[c]["out"]
    return outp
